# version 06 (final)
# speedup vs baseline: 1.3858x; 1.0149x over previous
import numpy as np
import ml_dtypes
import concourse.bass as bass
import concourse.mybir as mybir
from concourse.bass_utils import run_bass_kernel_spmd

F32 = mybir.dt.float32
BF16 = mybir.dt.bfloat16
I32 = mybir.dt.int32
U16 = mybir.dt.uint16
AF = mybir.ActivationFunctionType
OP = mybir.AluOpType
bf16 = ml_dtypes.bfloat16

NCORES = 8
T = 2048
E = 1024
EC = 8
NJ = 16
QMAX = 127.0
FC = 32
S0 = float(1.0 / np.sqrt(2.0 * 1024.0))
CSHIFT = 12.0
MC_C = 2048.0

_DBG = False
_UPTO = "full"

WQ0 = 0
QKT0 = 24576
VT0 = 32768
XN0 = 36928
BIGB_COLS = 41024
BS = BIGB_COLS


class Sem:
    def __init__(self, nc, es, name):
        self.h = es.enter_context(nc.semaphore(name))
        self.n = 0

    def inc(self, ins, k=1):
        ins.then_inc(self.h, k)
        self.n += k
        return self.n


class Ctx:
    pass


def _mk_ctx(upto):
    from contextlib import ExitStack
    c = Ctx()
    c.upto = upto
    nc = c.nc = bass.Bass("TRN2", target_bir_lowering=False, detect_race_conditions=False)
    es = c.es = ExitStack()

    c.d_xT = nc.dram_tensor("xT", [E, T], F32, kind="ExternalInput")
    c.d_out = nc.dram_tensor("outT", [E, T], F32, kind="ExternalOutput")
    c.d_wqkv = nc.dram_tensor("wqkv", [E, 3 * E], F32, kind="ExternalInput")
    c.d_bqk = nc.dram_tensor("bqk_c", [128, 16], F32, kind="ExternalInput")
    c.d_bv = nc.dram_tensor("bv_row", [1, 1024], F32, kind="ExternalInput")
    c.d_mask = nc.dram_tensor("maskcat", [128, 4096], BF16, kind="ExternalInput")
    c.d_ident = nc.dram_tensor("ident", [128, 128], F32, kind="ExternalInput")
    c.d_wg = nc.dram_tensor("wg", [2816, 1024], F32, kind="ExternalInput")
    c.d_wu = nc.dram_tensor("wu", [2816, 1024], F32, kind="ExternalInput")
    c.d_wd = nc.dram_tensor("wd", [1024, 2816], F32, kind="ExternalInput")
    c.d_wgT = nc.dram_tensor("wgT", [1024, 2816], F32, kind="ExternalInput")
    c.d_wuT = nc.dram_tensor("wuT", [1024, 2816], F32, kind="ExternalInput")
    c.d_wdT = nc.dram_tensor("wdT", [2816, 1024], F32, kind="ExternalInput")
    c.d_wqkv1 = nc.dram_tensor("wqkv1", [1024, 1536], F32, kind="ExternalInput")
    c.d_wfc1 = nc.dram_tensor("wfc1", [512, 1024], F32, kind="ExternalInput")
    c.d_bq1r = nc.dram_tensor("bq1r", [1, 1024], F32, kind="ExternalInput")
    c.d_bv1c = nc.dram_tensor("bv1c", [128, 4], F32, kind="ExternalInput")
    c.d_bfc1c = nc.dram_tensor("bfc1c", [128, 8], F32, kind="ExternalInput")
    c.d_gam1 = nc.dram_tensor("gam1c", [128, 8], F32, kind="ExternalInput")
    c.d_bet1 = nc.dram_tensor("bet1c", [128, 8], F32, kind="ExternalInput")
    c.d_q1 = nc.dram_tensor("q1u", [1024, 4096], U16, kind="ExternalInput")
    c.d_q2 = nc.dram_tensor("q2u", [4096, 1024], U16, kind="ExternalInput")
    c.d_s1 = nc.dram_tensor("s1", [8, 4096], F32, kind="ExternalInput")
    c.d_s2 = nc.dram_tensor("s2", [32, 1024], F32, kind="ExternalInput")
    c.d_b1c = nc.dram_tensor("b1c", [128, 32], F32, kind="ExternalInput")
    c.d_b2r = nc.dram_tensor("b2r", [1, 1024], F32, kind="ExternalInput")
    c.d_lngc = nc.dram_tensor("lngc", [128, 8], F32, kind="ExternalInput")
    c.d_lnbc = nc.dram_tensor("lnbc", [128, 8], F32, kind="ExternalInput")
    c.d_w1d = nc.dram_tensor("w1d_d", [1024, 4096], BF16)
    c.d_w2d = nc.dram_tensor("w2d_d", [4096, 1024], BF16)
    c.d_clipx = nc.dram_tensor("clipx", [1, 1], F32, kind="ExternalInput")
    c.d_clipd = nc.dram_tensor("clipd", [1, 1], F32, kind="ExternalInput")
    c.d_kwg = nc.dram_tensor("kwg_d", [1024, 2816], BF16)
    c.d_kwu = nc.dram_tensor("kwu_d", [1024, 2816], BF16)
    c.d_kwd = nc.dram_tensor("kwd_d", [8, 2816, 128], BF16)
    c.d_xsp = nc.dram_tensor("xsp_d", [1024, 2048], F32)
    c.d_cc1i = nc.dram_tensor("cc1i", [1, 8], F32)
    c.d_cc1o = nc.dram_tensor("cc1o", [1, 8], F32)
    c.d_cc2i = nc.dram_tensor("cc2i", [1, 32], F32)
    c.d_cc2o = nc.dram_tensor("cc2o", [1, 32], F32)
    c.d_dbg = nc.dram_tensor("dbg", [128, 4224], F32, kind="ExternalOutput")

    c.d_idb = nc.dram_tensor("idb", [128, 128], BF16, kind="ExternalInput")

    sb = lambda n, s, t: es.enter_context(nc.sbuf_tensor(n, s, t))
    c.X = sb("X", [128, EC * T], F32)
    c.BIGB = sb("BIGB", [128, BIGB_COLS], BF16)
    c.SB = sb("SB", [128, 4096], BF16)
    c.SBf = sb("SBf", [128, 2048], F32)
    c.SC = sb("SC", [128, 4096], F32)
    c.MASK = sb("MASK", [128, 4096], BF16)
    c.BQK = sb("BQK", [128, 16], F32)
    c.BVR = sb("BVR", [1, 1024], F32)
    c.ONES = sb("ONES", [128, 128], F32)
    c.CN = sb("CN", [128, 8], F32)
    c.IDT = sb("IDT", [128, 128], F32)
    c.SCL = sb("SCL", [128, 128], F32)
    c.IW = sb("IW", [128, 2048], I32)
    c.R32 = sb("R32", [1, 32], F32)
    c.CV1 = sb("CV1", [128, 48], F32)
    c.ONER = sb("ONER", [1, 512], F32)
    c.ONEC = sb("ONEC", [128, 1], BF16)
    c.IDB = sb("IDB", [128, 128], BF16)
    c.KST2 = sb("KST2", [128, 2048], BF16)

    c.psA = es.enter_context(nc.psum_tensor("psA", [128, 2048], F32))
    c.psB = es.enter_context(nc.psum_tensor("psB", [128, 2048], F32))

    for s in ("sD", "sW", "sU", "sV", "sT", "sS", "sG", "sC", "sQ", "sO"):
        setattr(c, s, Sem(nc, es, s))
    c.Xc = lambda ch: c.X[:, ch * T:(ch + 1) * T]
    return c


def p0_load(c):
    nc = c.nc
    with nc.Block(no_gpsimd_drain=True) as blk:
        @blk.sync
        def _(sync):
            for ch in range(EC):
                sync.dma_start(out=c.Xc(ch), in_=c.d_xT[ch * 128:(ch + 1) * 128, :]).then_inc(c.sD.h, 16)
            sync.dma_start(out=c.MASK[:, :], in_=c.d_mask[:, :]).then_inc(c.sW.h, 16)
            sync.dma_start(out=c.BQK[:, :], in_=c.d_bqk[:, :]).then_inc(c.sW.h, 16)
            sync.dma_start(out=c.BVR[:, :], in_=c.d_bv[:, :]).then_inc(c.sW.h, 16)
            sync.dma_start(out=c.IDT[:, :], in_=c.d_ident[:, :]).then_inc(c.sW.h, 16)
            sync.dma_start(out=c.IDB[:, :], in_=c.d_idb[:, :]).then_inc(c.sW.h, 16)
            c.sD.n += 8 * 16
            c.sW.n += 5 * 16

        @blk.gpsimd
        def _(g):
            g.memset(c.ONES[:, :], 1.0)
            g.memset(c.ONEC[:, :], 1.0)
            g.memset(c.ONER[:, :], 1.0)
            g.memset(c.CN[:, 0:1], 1e-5)
            g.memset(c.CN[:, 1:2], 1e-6)
            g.memset(c.CN[:, 2:3], -CSHIFT)
            g.memset(c.CN[:, 3:4], S0)
            g.memset(c.CN[:, 4:5], 1.0 / 127.0)
            g.memset(c.CN[:, 5:6], 127.0)
            g.memset(c.KST2[64:128, :], 0.0)
            ins = g.memset(c.BIGB[:, VT0:VT0 + 4160], 1.0)
            c.sG.inc(ins)
            for ch in range(EC):
                g.dma_start(out=c.BIGB[:, WQ0 + ch * 3072:WQ0 + (ch + 1) * 3072],
                            in_=c.d_wqkv[ch * 128:(ch + 1) * 128, :]).then_inc(c.sU.h, 16)
            c.sU.n += 8 * 16


def emit_ln(c, eps, x_src=None, sq_base=QKT0):
    nc = c.nc
    if x_src is None:
        x_src = c.Xc
    with nc.Block(no_gpsimd_drain=True) as blk:
        bT, bS = c.sT.n, c.sS.n

        @blk.scalar
        def _(s):
            s.wait_ge(c.sD.h, c.sD.n)
            for ch in range(EC):
                if ch >= 2:
                    s.wait_ge(c.sT.h, bT + 2 * (ch - 1))
                ins = s.activation(c.SB[:, (ch % 2) * 2048:(ch % 2) * 2048 + 2048],
                                   x_src(ch), AF.Copy, bias=0.0, scale=1.0)
                c.sS.inc(ins)
                ins = s.activation(c.BIGB[:, sq_base + (ch % 2) * 2048:sq_base + (ch % 2) * 2048 + 2048],
                                   x_src(ch), AF.Square, bias=0.0, scale=1.0)
                c.sS.inc(ins)

        @blk.tensor
        def _(t):
            for ch in range(EC):
                t.wait_ge(c.sS.h, bS + 2 * ch + 2)
                for nt in range(4):
                    ins = t.matmul(c.psA[0:1, nt * 512:(nt + 1) * 512], c.ONEC[:, :],
                                   c.SB[:, (ch % 2) * 2048 + nt * 512:(ch % 2) * 2048 + nt * 512 + 512],
                                   start=(ch == 0), stop=(ch == EC - 1), skip_group_check=True)
                c.sT.inc(ins)
                for nt in range(4):
                    ins = t.matmul(c.psB[0:1, nt * 512:(nt + 1) * 512], c.ONEC[:, :],
                                   c.BIGB[:, sq_base + (ch % 2) * 2048 + nt * 512:
                                           sq_base + (ch % 2) * 2048 + nt * 512 + 512],
                                   start=(ch == 0), stop=(ch == EC - 1), skip_group_check=True)
                c.sT.inc(ins)

        @blk.vector
        def _(v):
            v.wait_ge(c.sT.h, bT + 2 * EC)
            ins = v.tensor_scalar(c.SC[0:1, 0:2048], c.psA[0:1, 0:2048], 1.0 / E, None, OP.mult)
            ins = v.tensor_scalar(c.SC[0:1, 2048:4096], c.psB[0:1, 0:2048], 1.0 / E, None, OP.mult)
            ins = v.tensor_tensor(c.SBf[0:1, 0:2048], c.SC[0:1, 0:2048], c.SC[0:1, 0:2048], OP.mult)
            ins = v.tensor_tensor(c.SC[0:1, 2048:4096], c.SC[0:1, 2048:4096], c.SBf[0:1, 0:2048], OP.subtract)
            c.sV.inc(ins)

        @blk.scalar
        def _(s):
            s.wait_ge(c.sV.h, c.sV.n)
            ins = s.activation(c.SBf[0:1, 0:2048], c.SC[0:1, 2048:4096], AF.Sqrt, bias=c.CN[0:1, 0:1] if eps == 1e-5 else c.CN[0:1, 1:2], scale=1.0)
            c.sS.inc(ins)

        @blk.vector
        def _(v):
            v.wait_ge(c.sS.h, c.sS.n)
            ins = v.reciprocal(c.SC[0:1, 2048:4096], c.SBf[0:1, 0:2048])
            ins = v.tensor_tensor(c.SBf[0:1, 0:2048], c.SC[0:1, 0:2048], c.SC[0:1, 2048:4096], OP.mult)
            c.sV.inc(ins)

        @blk.tensor
        def _(t):
            t.wait_ge(c.sV.h, c.sV.n)
            for nt in range(4):
                ins = t.matmul(c.psA[:, nt * 512:(nt + 1) * 512], c.ONES[0:1, :],
                               c.SBf[0:1, nt * 512:nt * 512 + 512], start=True, stop=True)
            c.sT.inc(ins)
            for nt in range(4):
                ins = t.matmul(c.psB[:, nt * 512:(nt + 1) * 512], c.ONES[0:1, :],
                               c.SC[0:1, 2048 + nt * 512:2048 + nt * 512 + 512], start=True, stop=True)
            c.sT.inc(ins)

        @blk.vector
        def _(v):
            v.wait_ge(c.sT.h, c.sT.n)
            ins = v.tensor_copy(c.SC[:, 0:2048], c.psA[:, 0:2048])
            c.sV.inc(ins)
            ins = v.tensor_copy(c.SC[:, 2048:4096], c.psB[:, 0:2048])
            c.sV.inc(ins)


def emit_xn_cols(c, v, lo, w, dst, dtile=None):
    if dtile is None:
        dtile = c.BIGB
    for ch in range(EC):
        ins = v.tensor_tensor(c.SBf[:, 0:w], c.Xc(ch)[:, lo:lo + w], c.SC[:, 2048 + lo:2048 + lo + w], OP.mult)
        ins = v.tensor_tensor(dtile[:, dst + ch * w:dst + (ch + 1) * w],
                              c.SBf[:, 0:w], c.SC[:, lo:lo + w], OP.subtract)
        c.sV.inc(ins)


def p2_l0_attn(c):
    nc = c.nc
    X, BIGB, SB, SBf, MASK = c.X, c.BIGB, c.SB, c.SBf, c.MASK
    psA, psB = c.psA, c.psB
    sV, sT, sS, sQ, sU = c.sV, c.sT, c.sS, c.sQ, c.sU
    IWf = c.IW.bitcast(F32)

    def ihj(i):
        return i // 32, (i // 16) % 2, i % 16

    T_ops, S_ops, V_ops, Q_ops, G_ops = [], [], [], [], []
    for g in range(4):
        for j in range(NJ):
            T_ops.append(("qkv", g, j))
            if g > 0 and j == 3:
                T_ops.append(("bc", 8 * g - 1))
        for idx in range(16):
            T_ops.append(("vmm", g, idx))
        due = {20 + 16 * k: 8 * g + k for k in range(7)}
        for p in range(129):
            if p < 128:
                i = 128 * g + p
                T_ops += [("sc0", i), ("mk0", i), ("sc1", i), ("mk1", i)]
            if p >= 1:
                i = 128 * g + p - 1
                T_ops += [("pv0", i), ("pv1", i)]
            if p in due:
                T_ops.append(("bc", due[p]))
    T_ops.append(("bc", 31))

    for g in range(4):
        for j in range(NJ):
            S_ops.append(("evq", g, j))
        for p in range(128):
            i = 128 * g + p
            S_ops += [("expA", i), ("expB", i)]

    for ch in range(EC):
        V_ops.append(("xnc", 0, ch))
    for g in range(4):
        for idx in range(16):
            V_ops.append(("evv", g, idx))
        for bl in range(8):
            b = 8 * g + bl
            V_ops += [("evac", b), ("recip", b)]
            if g < 3:
                V_ops.append(("xnc", g + 1, bl))
            V_ops += [("t2", b), ("xadd", b)]

    Q_ops = [("shift", b) for b in range(32)]
    G_ops = [("kcp", h) for h in range(16)]

    INC = {"qkv": 1, "vmm": 1, "sc0": 1, "mk0": 1, "sc1": 1, "mk1": 1,
           "pv0": 1, "pv1": 1, "bc": 1, "evq": 1, "expA": 1, "expB": 1,
           "xnc": 1, "evv": 1, "evac": 1, "recip": 1, "t2": 1, "xadd": 1,
           "kcp": 16, "shift": 16}
    done = {}
    fin = {}
    for eng, ops, s0 in (("T", T_ops, sT.n), ("S", S_ops, sS.n),
                         ("V", V_ops, sV.n), ("Q", Q_ops, sQ.n),
                         ("G", G_ops, sU.n)):
        n = s0
        for op in ops:
            n += INC[op[0]]
            done[op] = n
        fin[eng] = n

    def kst(h, j2):
        if h % 2 == 0:
            return SB[0:128, 2048 + 128 * j2:2048 + 128 * j2 + 128]
        return c.KST2[0:128, 128 * j2:128 * j2 + 128]

    with nc.Block(no_gpsimd_drain=True) as blk:
        @blk.tensor
        def _(t):
            t.wait_ge(c.sW.h, c.sW.n)
            t.wait_ge(c.sU.h, fin["G"] - 16 * 16)
            t.wait_ge(c.sG.h, 1)
            for op in T_ops:
                k = op[0]
                if k == "qkv":
                    g, j = op[1], op[2]
                    if j == 0:
                        t.wait_ge(sV.h, done[("xnc", g, 7)])
                        if g > 0:
                            t.wait_ge(sV.h, done[("t2", 8 * g - 2)])
                    if j >= 2:
                        t.wait_ge(sS.h, done[("evq", g, j - 2)])
                    pslot = psB[:, 1024 + (j % 2) * 512:1024 + (j % 2) * 512 + 512]
                    for ch in range(EC):
                        ins = t.matmul(pslot,
                                       BIGB[:, WQ0 + ch * 3072 + 192 * j:WQ0 + ch * 3072 + 192 * j + 128],
                                       BIGB[:, XN0 + ch * 512:XN0 + (ch + 1) * 512],
                                       start=(ch == 0), stop=(ch == EC - 1))
                    c.sT.inc(ins)
                elif k == "vmm":
                    g, idx = op[1], op[2]
                    hg, jg = idx // 4, idx % 4
                    if idx == 0:
                        t.wait_ge(sS.h, done[("evq", g, NJ - 1)])
                    if idx >= 2:
                        t.wait_ge(sV.h, done[("evv", g, idx - 2)])
                    pslot = psB[:, 1024 + (idx % 2) * 512:1024 + (idx % 2) * 512 + 256]
                    for ch in range(EC):
                        rhs = bass.AP(BIGB, WQ0 + ch * 3072 + 192 * (4 * jg) + 128,
                                      [[BS, 128], [192, 4], [1, 64]])
                        ins = t.matmul(pslot,
                                       BIGB[:, XN0 + ch * 512 + 128 * hg:XN0 + ch * 512 + 128 * hg + 128],
                                       rhs, start=(ch == 0), stop=False)
                    ins = t.matmul(pslot, c.ONES[0:1, :], c.BVR[0:1, 256 * jg:256 * jg + 256],
                                   start=False, stop=True)
                    c.sT.inc(ins)
                elif k == "sc0":
                    i = op[1]
                    h, hv, j2 = ihj(i)
                    sl = (i % 2) * 1024
                    if ("expA", i - 2) in done:
                        t.wait_ge(sS.h, done[("expA", i - 2)])
                    if i % 32 == 0:
                        t.wait_ge(sU.h, done[("kcp", h)])
                    if i % 128 == 0 and i > 0:
                        t.wait_ge(sV.h, done[("t2", 8 * (i // 128) - 1)])
                    rhs = bass.AP(BIGB, QKT0 + 512 * (8 * hv) + 128 * (h % 4),
                                  [[BS, 128], [512, 4], [1, 128]])
                    ins = t.matmul(psA[:, sl:sl + 512], kst(h, j2), rhs,
                                   start=True, stop=False, skip_group_check=True)
                    c.sT.inc(ins)
                elif k == "mk0":
                    i = op[1]
                    h, hv, j2 = ihj(i)
                    sl = (i % 2) * 1024
                    moff = (16 - j2) * 128 + hv * 1024
                    ins = t.matmul(psA[:, sl:sl + 512], c.IDB[:, :], MASK[:, moff:moff + 512],
                                   start=False, stop=True, skip_group_check=True)
                    c.sT.inc(ins)
                elif k == "sc1":
                    i = op[1]
                    h, hv, j2 = ihj(i)
                    sl = (i % 2) * 1024
                    if ("expB", i - 2) in done:
                        t.wait_ge(sS.h, done[("expB", i - 2)])
                    rhs = bass.AP(BIGB, QKT0 + 512 * (8 * hv + 4) + 128 * (h % 4),
                                  [[BS, 128], [512, 4], [1, 128]])
                    ins = t.matmul(psA[:, sl + 512:sl + 1024], kst(h, j2), rhs,
                                   start=True, stop=False, skip_group_check=True)
                    c.sT.inc(ins)
                elif k == "mk1":
                    i = op[1]
                    h, hv, j2 = ihj(i)
                    sl = (i % 2) * 1024
                    moff = (16 - j2) * 128 + hv * 1024
                    ins = t.matmul(psA[:, sl + 512:sl + 1024], c.IDB[:, :],
                                   MASK[:, moff + 512:moff + 1024],
                                   start=False, stop=True, skip_group_check=True)
                    c.sT.inc(ins)
                elif k == "pv0":
                    i = op[1]
                    h, hv, j2 = ihj(i)
                    sl = (i % 2) * 1024
                    t.wait_ge(sS.h, done[("expA", i)])
                    if j2 == 0 and i >= 16:
                        t.wait_ge(sV.h, done[("evac", i // 16 - 1)])
                    lhsv = BIGB[:, VT0 + 1040 * (h % 4) + 65 * j2:VT0 + 1040 * (h % 4) + 65 * j2 + 65]
                    ins = t.matmul(psB[0:65, 0:512], lhsv, SB[:, sl:sl + 512],
                                   start=(j2 == 0), stop=(j2 == NJ - 1), skip_group_check=True)
                    c.sT.inc(ins)
                elif k == "pv1":
                    i = op[1]
                    h, hv, j2 = ihj(i)
                    sl = (i % 2) * 1024
                    t.wait_ge(sS.h, done[("expB", i)])
                    lhsv = BIGB[:, VT0 + 1040 * (h % 4) + 65 * j2:VT0 + 1040 * (h % 4) + 65 * j2 + 65]
                    ins = t.matmul(psB[0:65, 512:1024], lhsv, SB[:, sl + 512:sl + 1024],
                                   start=(j2 == 0), stop=(j2 == NJ - 1), skip_group_check=True)
                    c.sT.inc(ins)
                elif k == "bc":
                    b = op[1]
                    g = b // 8
                    lastb = (b % 8 == 7)
                    t.wait_ge(sV.h, done[("recip", b)])
                    if lastb:
                        t.wait_ge(sS.h, done[("expB", 128 * g + 126)])
                    elif b % 8 == 0:
                        t.wait_ge(sS.h, done[("evq", g, NJ - 1)])
                        t.wait_ge(sV.h, done[("evv", g, NJ - 1)])
                    else:
                        t.wait_ge(sV.h, done[("t2", b - 1)])
                    for nt in range(2):
                        tgt = psA[0:64, nt * 512:nt * 512 + 512] if lastb else \
                            psB[0:64, 1024 + nt * 512:1024 + nt * 512 + 512]
                        ins = t.matmul(tgt, c.ONES[64:65, 0:64],
                                       SBf[64:65, 1024 + nt * 512:1024 + nt * 512 + 512],
                                       start=True, stop=True, skip_group_check=True)
                    c.sT.inc(ins)

        @blk.scalar
        def _(s):
            s.wait_ge(c.sW.h, c.sW.n)
            for op in S_ops:
                k = op[0]
                if k == "evq":
                    g, j = op[1], op[2]
                    s.wait_ge(sT.h, done[("qkv", g, j)])
                    if g > 0 and j == 0:
                        s.wait_ge(sU.h, done[("kcp", 4 * (g - 1) + 3)])
                    pslot = psB[:, 1024 + (j % 2) * 512:1024 + (j % 2) * 512 + 512]
                    ins = s.activation(BIGB[:, QKT0 + 512 * j:QKT0 + 512 * j + 512], pslot,
                                       AF.Identity, bias=c.BQK[:, j:j + 1], scale=1.0)
                    c.sS.inc(ins)
                elif k == "expA":
                    i = op[1]
                    sl = (i % 2) * 1024
                    s.wait_ge(sT.h, done[("mk0", i)])
                    if ("pv0", i - 2) in done:
                        s.wait_ge(sT.h, done[("pv0", i - 2)])
                    ins = s.activation(SB[:, sl:sl + 512], psA[:, sl:sl + 512],
                                       AF.Exp, bias=c.CN[:, 2:3], scale=c.CN[:, 3:4])
                    c.sS.inc(ins)
                elif k == "expB":
                    i = op[1]
                    sl = (i % 2) * 1024
                    s.wait_ge(sT.h, done[("mk1", i)])
                    if ("pv1", i - 2) in done:
                        s.wait_ge(sT.h, done[("pv1", i - 2)])
                    ins = s.activation(SB[:, sl + 512:sl + 1024], psA[:, sl + 512:sl + 1024],
                                       AF.Exp, bias=c.CN[:, 2:3], scale=c.CN[:, 3:4])
                    c.sS.inc(ins)

        @blk.vector
        def _(v):
            v.wait_ge(c.sW.h, c.sW.n)
            for op in V_ops:
                k = op[0]
                if k == "xnc":
                    g, ch = op[1], op[2]
                    if ch == 0 and g > 0:
                        v.wait_ge(sT.h, done[("vmm", g - 1, 15)])
                    lo = 512 * g
                    ins = v.tensor_tensor(IWf[:, 0:512], c.Xc(ch)[:, lo:lo + 512],
                                          c.SC[:, 2048 + lo:2048 + lo + 512], OP.mult)
                    ins = v.tensor_tensor(BIGB[:, XN0 + ch * 512:XN0 + ch * 512 + 512],
                                          IWf[:, 0:512], c.SC[:, lo:lo + 512], OP.subtract)
                    c.sV.inc(ins)
                elif k == "evv":
                    g, idx = op[1], op[2]
                    hg, jg = idx // 4, idx % 4
                    v.wait_ge(sT.h, done[("vmm", g, idx)])
                    slot = psB[:, 1024 + (idx % 2) * 512:1024 + (idx % 2) * 512 + 256]
                    dst = bass.AP(BIGB, VT0 + 1040 * hg + 65 * (4 * jg), [[BS, 128], [65, 4], [1, 64]])
                    ins = v.tensor_copy(dst, slot)
                    c.sV.inc(ins)
                elif k == "evac":
                    b = op[1]
                    v.wait_ge(sT.h, done[("pv1", 16 * b + 15)])
                    if b > 0:
                        v.wait_ge(sQ.h, done[("shift", b - 1)])
                    ins = v.tensor_copy(SBf[0:65, 0:1024], psB[0:65, 0:1024])
                    c.sV.inc(ins)
                elif k == "recip":
                    b = op[1]
                    ins = v.reciprocal(SBf[64:65, 1024:2048], SBf[64:65, 0:1024])
                    c.sV.inc(ins)
                elif k == "t2":
                    b = op[1]
                    v.wait_ge(sT.h, done[("bc", b)])
                    src1 = psA[0:64, 0:1024] if b % 8 == 7 else psB[0:64, 1024:2048]
                    ins = v.tensor_tensor(SBf[0:64, 1024:2048], SBf[0:64, 0:1024], src1, OP.mult)
                    c.sV.inc(ins)
                elif k == "xadd":
                    b = op[1]
                    h, hv = b // 2, b % 2
                    v.wait_ge(sQ.h, done[("shift", b)])
                    dA = bass.AP(X, (hv * 4) * T + 128 * h, [[EC * T, 64], [T, 4], [1, 128]])
                    sA = bass.AP(SBf, 1024, [[2048, 64], [256, 4], [1, 128]])
                    ins = v.tensor_tensor(dA, dA, sA, OP.add)
                    dB = bass.AP(X, 64 * EC * T + (hv * 4) * T + 128 * h, [[EC * T, 64], [T, 4], [1, 128]])
                    sB = bass.AP(SBf, 64 * 2048, [[2048, 64], [128, 4], [1, 128]])
                    ins = v.tensor_tensor(dB, dB, sB, OP.add)
                    c.sV.inc(ins)

        @blk.sync
        def _(sync):
            for op in Q_ops:
                b = op[1]
                sync.wait_ge(sV.h, done[("t2", b)])
                src2 = bass.AP(SBf, 1024 + 128, [[2048, 64], [256, 4], [1, 128]])
                sync.dma_start(out=SBf[64:128, 0:512], in_=src2).then_inc(sQ.h, 16)
                sQ.n += 16

        @blk.gpsimd
        def _(ge):
            ge.memset(SB[64:128, 2048:4096], 0.0)
            for op in G_ops:
                h = op[1]
                g = h // 4
                ge.wait_ge(sS.h, done[("evq", g, NJ - 1)])
                if h >= 2:
                    ge.wait_ge(sT.h, done[("sc1", 32 * h - 33)])
                src = bass.AP(BIGB, 64 * BS + QKT0 + 128 * (h % 4), [[BS, 64], [512, 16], [1, 128]])
                dst = (SB[0:64, 2048:4096] if h % 2 == 0 else c.KST2[0:64, 0:2048])
                ge.dma_start(out=dst, in_=src).then_inc(sU.h, 16)
                sU.n += 16

    assert (sT.n, sS.n, sV.n, sQ.n, sU.n) == (fin["T"], fin["S"], fin["V"], fin["Q"], fin["G"])


def _store_out(c):
    nc = c.nc
    with nc.Block(no_gpsimd_drain=True) as blk:
        @blk.sync
        def _(sync):
            for s in (c.sV, c.sT, c.sS):
                if s.n:
                    sync.wait_ge(s.h, s.n)
            for ch in range(EC):
                sync.dma_start(out=c.d_out[ch * 128:(ch + 1) * 128, :], in_=c.Xc(ch)).then_inc(c.sO.h, 16)
            c.sO.n += 8 * 16
            sync.wait_ge(c.sO.h, c.sO.n)


def build_nc(upto="full"):
    c = _mk_ctx(upto)
    with c.es:
        p0_load(c)
        emit_ln(c, 1e-5)
        if upto == "ln1":
            with c.nc.Block(no_gpsimd_drain=True) as blk:
                @blk.vector
                def _(v):
                    v.wait_ge(c.sT.h, c.sT.n)
                    for ch in range(EC):
                        for half in range(2):
                            lo = half * 1024
                            ins = v.tensor_tensor(c.SBf[:, 0:1024], c.Xc(ch)[:, lo:lo + 1024],
                                                  c.SC[:, 2048 + lo:2048 + lo + 1024], OP.mult)
                            ins = v.tensor_tensor(c.Xc(ch)[:, lo:lo + 1024], c.SBf[:, 0:1024],
                                                  c.SC[:, lo:lo + 1024], OP.subtract)
                            c.sV.inc(ins)
            _store_out(c)
            return c.nc
        p2_l0_attn(c)
        if upto == "l0a":
            _store_out(c)
            return c.nc
        p3_prep_weights(c)
        c.prep_q = c.sQ.n
        p3_l0_ffn(c, upto)
        if upto.startswith("l0f"):
            pass
        else:
            p4_l1_attn(c, upto)
            if not upto.startswith("l1a"):
                p5_l1_ffn(c, upto)
        if _DBG and upto.startswith("l0f"):
            with c.nc.Block(no_gpsimd_drain=True) as blk:
                @blk.sync
                def _(sync):
                    sync.wait_ge(c.sV.h, c.sV.n)
                    sync.dma_start(out=c.d_dbg[:, 0:128], in_=c.SCL[:, :]).then_inc(c.sO.h, 16)
                    c.sO.n += 16

                @blk.gpsimd
                def _(g):
                    g.wait_ge(c.sV.h, c.sV.n)
                    g.dma_start(out=c.d_dbg[:, 128:2176], in_=c.d_kwg[0:128, 0:2048]).then_inc(c.sO.h, 16)
                    g.dma_start(out=c.d_dbg[:, 2176:4224],
                                in_=c.X[:, :].bitcast(BF16)[:, 0:2048]).then_inc(c.sO.h, 16)
                    c.sO.n += 32
        _store_out(c)
    return c.nc


def _host_prep(inputs):
    f32 = np.float32
    b = np.asarray(inputs["L0_attn_qkv_bias"], f32)[0]
    prep = {}
    prep["wqkv"] = np.ascontiguousarray(np.asarray(inputs["L0_attn_qkv_kernel"], f32))
    prep["bqk_c"] = np.stack([b[192 * j:192 * j + 128] for j in range(NJ)], axis=1).astype(f32)
    prep["bv_row"] = np.ascontiguousarray(
        np.concatenate([b[192 * j + 128:192 * j + 192] for j in range(NJ)])[None, :].astype(f32))
    mc = np.zeros((128, 4096), dtype=np.float32)
    tri1 = np.triu(np.ones((128, 128), np.float32), 1)
    tri0 = np.triu(np.ones((128, 128), np.float32), 0)
    for jb in range(16):
        mc[:, jb * 128:(jb + 1) * 128] = tri1
    for jb in range(16, 32):
        mc[:, jb * 128:(jb + 1) * 128] = tri0
    prep["maskcat"] = (MC_C * (mc - 1.0)).astype(bf16)
    prep["ident"] = np.eye(128, dtype=f32)
    prep["idb"] = np.eye(128, dtype=f32).astype(bf16)
    prep["wg"] = np.ascontiguousarray(np.asarray(inputs["L0_ffn_w_gate"], f32))
    prep["wu"] = np.ascontiguousarray(np.asarray(inputs["L0_ffn_w_up"], f32))
    prep["wd"] = np.ascontiguousarray(np.asarray(inputs["L0_ffn_w_down"], f32))
    prep["wgT"] = np.ascontiguousarray(prep["wg"].T)
    prep["wuT"] = np.ascontiguousarray(prep["wu"].T)
    prep["wdT"] = np.ascontiguousarray(prep["wd"].T)
    b1 = np.asarray(inputs["L1_attn_b_qkv"], f32)
    prep["wqkv1"] = np.ascontiguousarray(np.asarray(inputs["L1_attn_w_qkv"], f32))
    prep["wfc1"] = np.ascontiguousarray(np.asarray(inputs["L1_attn_w_fc"], f32))
    prep["bq1r"] = np.ascontiguousarray(b1[None, 0:1024])
    prep["bv1c"] = np.ascontiguousarray(b1[1024:1536].reshape(4, 128).T)
    prep["bfc1c"] = np.ascontiguousarray(np.asarray(inputs["L1_attn_b_fc"], f32).reshape(8, 128).T)
    prep["gam1c"] = np.ascontiguousarray(np.asarray(inputs["L1_attn_gamma"], f32).reshape(8, 128).T)
    prep["bet1c"] = np.ascontiguousarray(np.asarray(inputs["L1_attn_beta"], f32).reshape(8, 128).T)
    prep["q1u"] = np.ascontiguousarray(np.asarray(inputs["L1_ffn_fc1_q"]).view(np.uint16).reshape(1024, 4096, 2)[:, :, 0])
    prep["q2u"] = np.ascontiguousarray(np.asarray(inputs["L1_ffn_fc2_q"]).view(np.uint16).reshape(4096, 1024, 2)[:, :, 0])
    prep["s1"] = np.ascontiguousarray(np.asarray(inputs["L1_ffn_fc1_scales"], f32))
    prep["s2"] = np.ascontiguousarray(np.asarray(inputs["L1_ffn_fc2_scales"], f32))
    prep["b1c"] = np.ascontiguousarray(np.asarray(inputs["L1_ffn_fc1_bias"], f32).reshape(32, 128).T)
    prep["b2r"] = np.ascontiguousarray(np.asarray(inputs["L1_ffn_fc2_bias"], f32)[None, :])
    prep["lngc"] = np.ascontiguousarray(np.asarray(inputs["L1_ffn_ln_gamma"], f32).reshape(8, 128).T)
    prep["lnbc"] = np.ascontiguousarray(np.asarray(inputs["L1_ffn_ln_beta"], f32).reshape(8, 128).T)
    prep["clipx"] = np.asarray(inputs["L0_ffn_clip_x"], f32).reshape(1, 1)
    prep["clipd"] = np.asarray(inputs["L0_ffn_clip_down"], f32).reshape(1, 1)
    return prep


def kernel(**inputs):
    x = np.asarray(inputs["L0_attn_inputs"], np.float32)
    prep = _host_prep(inputs)
    nc = build_nc(_UPTO)
    in_maps = []
    for ci in range(NCORES):
        m = dict(prep)
        m["xT"] = np.ascontiguousarray(x[ci].T)
        in_maps.append(m)
    res = run_bass_kernel_spmd(nc, in_maps, core_ids=list(range(NCORES)))
    return np.stack([np.ascontiguousarray(r["outT"].T) for r in res.results], axis=0)


II = 2816
ICN = 22
KX0 = 0
HT0 = 16384
WUP0 = 28672
PRT0 = 0
PRK0 = 5632
PRN0 = 11264
PRS0 = 13312


RNE_C = 12582912.0


def _quant4(c, v, src_ap, dst_ap, w, rc_col_or_bc, bcast):
    if bcast:
        ins = v.tensor_tensor(c.SBf[:, 0:w], src_ap, rc_col_or_bc, OP.mult)
    else:
        ins = v.tensor_scalar(c.SBf[:, 0:w], src_ap, rc_col_or_bc, None, OP.mult)
    ins = v.tensor_scalar(c.SBf[:, 0:w], c.SBf[:, 0:w], 127.49, -128.49, OP.min, OP.max)
    ins = v.tensor_scalar(dst_ap, c.SBf[:, 0:w], RNE_C, -RNE_C, OP.add, OP.add)
    return ins


def _prep_scale_phase(c, blk, cfg):
    nc = c.nc
    B = c.BIGB
    d_nat, natc, nat_w, prt, scol, scr = (cfg["nat"], cfg["natc"], cfg["nat_w"],
                                          cfg["prt"], cfg["scol"], cfg["scr"])
    nloc = natc * 128
    bU, bV, bT, bQ = c.sU.n, c.sV.n, c.sT.n, c.sQ.n

    @blk.gpsimd
    def _(g):
        for ic in range(natc):
            if ic >= 2:
                g.wait_ge(c.sV.h, bV + ic - 1)
            ins = g.dma_start(out=B[:, prt + (ic % 2) * II:prt + (ic % 2) * II + nat_w],
                              in_=d_nat[ic * 128:(ic + 1) * 128, :])
            c.sU.inc(ins, 16)

    @blk.vector
    def _(v):
        for ic in range(natc):
            v.wait_ge(c.sU.h, bU + 16 * (ic + 1))
            ins = v.reduce_max(c.SCL[:, scol + ic:scol + ic + 1],
                               B[:, prt + (ic % 2) * II:prt + (ic % 2) * II + nat_w],
                               axis=mybir.AxisListType.X, apply_absolute_value=True)
            c.sV.inc(ins)

    @blk.tensor
    def _(t):
        t.wait_ge(c.sV.h, bV + natc)
        ins = t.transpose(c.psA[0:natc, 0:128], c.SCL[:, scol:scol + natc], c.IDT[:, :])
        c.sT.inc(ins)

    @blk.vector
    def _(v):
        v.wait_ge(c.sT.h, bT + 1)
        v.wait_ge(c.sQ.h, bQ)
        ins = v.tensor_copy(c.SBf[0:natc, 0:128], c.psA[0:natc, 0:128])
        c.sV.inc(ins)

    @blk.sync
    def _(sync):
        sync.wait_ge(c.sV.h, bV + natc + 1)
        ins = sync.dma_start(out=c.SC[0:1, scr:scr + nloc], in_=c.SBf[0:natc, 0:128])
        c.sQ.inc(ins, 16)

    @blk.vector
    def _(v):
        v.wait_ge(c.sQ.h, bQ + 16)
        ins = v.reciprocal(c.SC[0:1, scr:scr + nloc], c.SC[0:1, scr:scr + nloc])
        c.sV.inc(ins)

    @blk.scalar
    def _(s):
        s.wait_ge(c.sV.h, c.sV.n)
        ins = s.activation(c.SC[0:1, scr:scr + nloc], c.SC[0:1, scr:scr + nloc],
                           AF.Identity, bias=0.0, scale=c.CN[0:1, 5:6])
        c.sS.inc(ins)

    @blk.tensor
    def _(t):
        t.wait_ge(c.sV.h, c.sV.n)
        t.wait_ge(c.sS.h, c.sS.n)
        off = 0
        while off < nloc:
            wdt = min(512, nloc - off)
            dstp = c.psA[:, off:off + wdt] if off + wdt <= 2048 else c.psB[:, off - 2048:off - 2048 + wdt]
            ins = t.matmul(dstp, c.ONES[0:1, :], c.SC[0:1, scr + off:scr + off + wdt],
                           start=True, stop=True)
            off += wdt
        c.sT.inc(ins)

    @blk.vector
    def _(v):
        v.wait_ge(c.sT.h, c.sT.n)
        ins = v.tensor_copy(c.SC[:, scr:scr + min(nloc, 2048)], c.psA[:, 0:min(nloc, 2048)])
        if nloc > 2048:
            ins = v.tensor_copy(c.SC[:, scr + 2048:scr + nloc], c.psB[:, 0:nloc - 2048])
        c.sV.inc(ins)


def _prep_quant_phase(c, blk, cfg):
    nc = c.nc
    B = c.BIGB
    d_T, d_out, nchunks, tw, prt, prk, scr = (cfg["trn"], cfg["out"], cfg["nchunks"],
                                              cfg["tw"], cfg["prt"], cfg["prk"], cfg["scr"])
    bU, bV, bQ = c.sU.n, c.sV.n, c.sQ.n

    @blk.gpsimd
    def _(g):
        for hc in range(nchunks):
            if hc >= 2:
                g.wait_ge(c.sV.h, bV + 2 * (hc - 1))
            ins = g.dma_start(out=B[:, prt + (hc % 2) * II:prt + (hc % 2) * II + tw],
                              in_=d_T[hc * 128:(hc + 1) * 128, :])
            c.sU.inc(ins, 16)

    @blk.vector
    def _(v):
        for hc in range(nchunks):
            v.wait_ge(c.sU.h, bU + 16 * (hc + 1))
            if hc >= 2:
                v.wait_ge(c.sQ.h, bQ + 16 * (hc - 1))
            off = 0
            while off < tw:
                wdt = min(2048, tw - off)
                src = B[:, prt + (hc % 2) * II + off:prt + (hc % 2) * II + off + wdt]
                dst = B[:, prk + (hc % 2) * II + off:prk + (hc % 2) * II + off + wdt]
                ins = _quant4(c, v, src, dst, wdt, c.SC[:, scr + off:scr + off + wdt], True)
                off += wdt
            c.sV.inc(ins, 2)

    @blk.sync
    def _(sync):
        for hc in range(nchunks):
            sync.wait_ge(c.sV.h, bV + 2 * (hc + 1))
            src = B[:, prk + (hc % 2) * II:prk + (hc % 2) * II + tw]
            if not cfg["blocked"]:
                ins = sync.dma_start(out=d_out[hc * 128:(hc + 1) * 128, :], in_=src)
            else:
                dstap = bass.AP(d_out, hc * 128 * 128, [[128, 128], [II * 128, 8], [1, 128]])
                ins = sync.dma_start(out=dstap, in_=src)
            c.sQ.inc(ins, 16)


def p3_prep_weights(c):
    nc = c.nc
    cfgA = dict(nat=c.d_wg, trn=c.d_wgT, out=c.d_kwg, scol=0, natc=22, nchunks=8,
                tw=II, nat_w=1024, prt=0, prk=5632, scr=0, blocked=False)
    cfgB = dict(nat=c.d_wu, trn=c.d_wuT, out=c.d_kwu, scol=22, natc=22, nchunks=8,
                tw=II, nat_w=1024, prt=16384, prk=22016, scr=0, blocked=False)
    cfgC = dict(nat=c.d_wd, trn=c.d_wdT, out=c.d_kwd, scol=44, natc=8, nchunks=22,
                tw=1024, nat_w=II, prt=27648, prk=33280, scr=2816, blocked=True)
    with nc.Block(no_gpsimd_drain=True) as blk:
        _prep_scale_phase(c, blk, cfgA)
        _prep_scale_phase(c, blk, cfgC)
    with nc.Block(no_gpsimd_drain=True) as blk:
        _prep_quant_phase(c, blk, cfgA)
        _prep_quant_phase(c, blk, cfgC)
        _prep_scale_phase(c, blk, cfgB)
    with nc.Block(no_gpsimd_drain=True) as blk:
        _prep_quant_phase(c, blk, cfgB)


def p3_l0_ffn(c, upto='full'):
    nc = c.nc
    B = c.BIGB
    Xb = c.X[:, :].bitcast(BF16)

    def h_slot(ic):
        if ic < 16:
            return c.X[:, :].bitcast(BF16)[:, ic * 2048:(ic + 1) * 2048]
        return B[:, HT0 + (ic - 16) * 2048:HT0 + (ic - 16) * 2048 + 2048]

    emit_ln(c, 1e-5, sq_base=HT0)

    with nc.Block(no_gpsimd_drain=True) as blk:
        bV, bT, bQ, bS, bG = c.sV.n, c.sT.n, c.sQ.n, c.sS.n, c.sG.n

        @blk.vector
        def _(v):
            emit_xn_cols(c, v, 0, 2048, KX0)
            if _DBG:
                ins = v.tensor_copy(c.d_dbg[4:5, 0:2048] if False else c.SBf[0:1, 1024:1025], c.SBf[0:1, 1024:1025])
            for ch in range(EC):
                ins = v.reduce_max(c.SCL[:, 52 + ch:53 + ch],
                                   B[:, KX0 + ch * 2048:KX0 + (ch + 1) * 2048],
                                   axis=mybir.AxisListType.X, apply_absolute_value=True)
                c.sV.inc(ins)

        @blk.sync
        def _(sync):
            sync.wait_ge(c.sV.h, bV + EC)
            for ch in range(EC):
                sync.dma_start(out=c.d_xsp[ch * 128:(ch + 1) * 128, :], in_=c.Xc(ch)).then_inc(c.sQ.h, 16)
            c.sQ.n += 8 * 16
            c.spill_q = c.sQ.n

        @blk.tensor
        def _(t):
            t.wait_ge(c.sV.h, bV + 2 * EC)
            ins = t.transpose(c.psA[0:8, 0:128], c.SCL[:, 52:60], c.IDT[:, :])
            c.sT.inc(ins)

        @blk.scalar
        def _(s):
            s.wait_ge(c.sT.h, bT + 1)
            ins = s.activation(c.SBf[0:8, 0:128], c.psA[0:8, 0:128], AF.Identity, bias=0.0, scale=1.0)
            c.sS.inc(ins)

        @blk.vector
        def _(v):
            v.wait_ge(c.sS.h, c.sS.n)
            ins = v.reduce_max(c.SBf[0:8, 128:129], c.SBf[0:8, 0:128], axis=mybir.AxisListType.X)
            c.sV.inc(ins)

        @blk.sync
        def _(sync):
            sync.wait_ge(c.sV.h, c.sV.n)
            ins = sync.dma_start(out=c.R32[0:1, 0:8], in_=c.SBf[0:8, 128:129])
            c.sQ.inc(ins, 16)

        @blk.gpsimd
        def _(g):
            if _DBG:
                g.wait_ge(c.sV.h, bV + EC)
                g.dma_start(out=c.d_dbg[4:5, 0:2048], in_=c.BIGB[0:1, KX0:KX0 + 2048]).then_inc(c.sO.h, 16)
                c.sO.n += 16
            g.wait_ge(c.sQ.h, c.sQ.n)
            ins = g.dma_start(out=c.d_cc1i.ap(), in_=c.R32[0:1, 0:8])
            c.sU.inc(ins, 16)
            g.wait_ge(c.sU.h, c.sU.n)
            ins = g.collective_compute("AllReduce", OP.max, replica_groups=[list(range(NCORES))],
                                       ins=[c.d_cc1i.ap()], outs=[c.d_cc1o.ap()])
            c.sC.inc(ins)
            g.wait_ge(c.sC.h, c.sC.n)
            ins = g.dma_start(out=c.R32[0:1, 0:8], in_=c.d_cc1o.ap())
            c.sU.inc(ins, 16)
            ins = g.dma_start(out=c.CN[0:1, 6:7], in_=c.d_clipx.ap())
            c.sU.inc(ins, 16)
            ins = g.dma_start(out=c.CN[0:1, 7:8], in_=c.d_clipd.ap())
            c.sU.inc(ins, 16)

        @blk.vector
        def _(v):
            v.wait_ge(c.sU.h, c.sU.n)
            ins = v.reduce_max(c.SBf[0:1, 129:130], c.R32[0:1, 0:8], axis=mybir.AxisListType.X)
            c.sV.inc(ins)

        @blk.scalar
        def _(s):
            s.wait_ge(c.sV.h, c.sV.n)
            ins = s.activation(c.SBf[0:1, 131:132], c.CN[0:1, 6:7], AF.Sigmoid, bias=0.0, scale=1.0)
            c.sS.inc(ins)

        @blk.vector
        def _(v):
            v.wait_ge(c.sS.h, c.sS.n)
            ins = v.tensor_tensor(c.SBf[0:1, 132:133], c.SBf[0:1, 129:130], c.SBf[0:1, 131:132], OP.mult)
            c.sV.inc(ins)

        @blk.scalar
        def _(s):
            s.wait_ge(c.sV.h, c.sV.n)
            ins = s.activation(c.SBf[0:1, 134:135], c.SBf[0:1, 132:133], AF.Identity,
                               bias=0.0, scale=c.CN[0:1, 4:5])
            c.sS.inc(ins)

        @blk.vector
        def _(v):
            v.wait_ge(c.sS.h, c.sS.n)
            ins = v.reciprocal(c.SBf[0:1, 133:134], c.SBf[0:1, 134:135])
            c.sV.inc(ins)

        @blk.tensor
        def _(t):
            t.wait_ge(c.sV.h, c.sV.n)
            ins = t.matmul(c.psB[:, 1024:1025], c.ONES[0:1, :], c.SBf[0:1, 134:135], start=True, stop=True)
            ins = t.matmul(c.psB[:, 1025:1026], c.ONES[0:1, :], c.SBf[0:1, 133:134], start=True, stop=True)
            c.sT.inc(ins)

        @blk.scalar
        def _(s):
            s.wait_ge(c.sT.h, c.sT.n)
            ins = s.activation(c.SCL[:, 60:62], c.psB[:, 1024:1026], AF.Identity, bias=0.0, scale=1.0)
            c.sS.inc(ins)

        @blk.vector
        def _(v):
            v.wait_ge(c.sS.h, c.sS.n)
            ins = v.tensor_scalar(c.SCL[:, 62:84], c.SCL[:, 0:22], c.SCL[:, 60:61], 1.0 / 127.0, OP.mult, OP.mult)
            ins = v.tensor_scalar(c.SCL[:, 22:44],
                                  c.SCL[:, 22:44], c.SCL[:, 60:61], 1.0 / 127.0, OP.mult, OP.mult)
            c.sV.inc(ins)

        if _DBG:
            @blk.sync
            def _(sync):
                sync.wait_ge(c.sV.h, c.sV.n)
                sync.dma_start(out=c.d_dbg[1:2, 0:2048], in_=c.SBf[0:1, 0:2048]).then_inc(c.sO.h, 16)
                sync.dma_start(out=c.d_dbg[2:3, 0:32], in_=c.R32[0:1, 0:32]).then_inc(c.sO.h, 16)
                sync.dma_start(out=c.d_dbg[3:4, 0:128], in_=c.SCL[0:1, :]).then_inc(c.sO.h, 16)
                c.sO.n += 48

    if upto == "l0f_scales":
        return
    with nc.Block(no_gpsimd_drain=True) as blk:
        @blk.vector
        def _(v):
            for ch in range(EC):
                src = B[:, KX0 + ch * 2048:KX0 + (ch + 1) * 2048]
                ins = _quant4(c, v, src, src, 2048, c.SCL[:, 61:62], False)
                c.sV.inc(ins)

    if upto == "l0f_kx":
        return
    with nc.Block(no_gpsimd_drain=True) as blk:
        gm_done, ge_done, ue_done, h_done, wload = {}, {}, {}, {}, {}
        nU, nV, nT, nS = c.sU.n, c.sV.n, c.sT.n, c.sS.n
        for blkid in range(6):
            nU += 16 * 16
            wload[blkid] = nU
        for ic in range(ICN):
            nT += 1
            gm_done[(ic, "g")] = nT
            nS += 1
            ge_done[ic] = nS
            nT += 1
            gm_done[(ic, "u")] = nT
            nS += 1
            ue_done[ic] = nS
            nV += 2
            h_done[ic] = nV

        @blk.gpsimd
        def _(g):
            g.wait_ge(c.sQ.h, c.prep_q)
            for blkid in range(6):
                nic = min(4, ICN - blkid * 4)
                if blkid >= 2:
                    g.wait_ge(c.sT.h, gm_done[(blkid * 4 - 5, "u")])
                for hc in range(EC):
                    ins = g.dma_start(out=c.SC[:, :].bitcast(BF16)[:, (blkid % 2) * 4096 + hc * 512:
                                                                  (blkid % 2) * 4096 + hc * 512 + nic * 128],
                                      in_=c.d_kwg[hc * 128:(hc + 1) * 128, blkid * 512:blkid * 512 + nic * 128])
                    c.sU.inc(ins, 16)
                for hc in range(EC):
                    ins = g.dma_start(out=B[:, WUP0 + (blkid % 2) * 4096 + hc * 512:
                                            WUP0 + (blkid % 2) * 4096 + hc * 512 + nic * 128],
                                      in_=c.d_kwu[hc * 128:(hc + 1) * 128, blkid * 512:blkid * 512 + nic * 128])
                    c.sU.inc(ins, 16)

        @blk.tensor
        def _(t):
            for ic in range(ICN):
                blkid, icr = ic // 4, ic % 4
                t.wait_ge(c.sU.h, wload[blkid])
                if ic >= 1:
                    t.wait_ge(c.sS.h, ge_done[ic - 1])
                for nt in range(4):
                    for hc in range(EC):
                        lhs = c.SC[:, :].bitcast(BF16)[:, (blkid % 2) * 4096 + hc * 512 + icr * 128:
                                                       (blkid % 2) * 4096 + hc * 512 + icr * 128 + 128]
                        ins = t.matmul(c.psA[:, nt * 512:(nt + 1) * 512], lhs,
                                       B[:, KX0 + hc * 2048 + nt * 512:KX0 + hc * 2048 + nt * 512 + 512],
                                       start=(hc == 0), stop=(hc == EC - 1))
                c.sT.inc(ins)
                if ic >= 1:
                    t.wait_ge(c.sS.h, ue_done[ic - 1])
                for nt in range(4):
                    for hc in range(EC):
                        lhs = B[:, WUP0 + (blkid % 2) * 4096 + hc * 512 + icr * 128:
                                WUP0 + (blkid % 2) * 4096 + hc * 512 + icr * 128 + 128]
                        ins = t.matmul(c.psB[:, nt * 512:(nt + 1) * 512], lhs,
                                       B[:, KX0 + hc * 2048 + nt * 512:KX0 + hc * 2048 + nt * 512 + 512],
                                       start=(hc == 0), stop=(hc == EC - 1))
                c.sT.inc(ins)

        @blk.scalar
        def _(s):
            for ic in range(ICN):
                s.wait_ge(c.sT.h, gm_done[(ic, "g")])
                if ic >= 1:
                    s.wait_ge(c.sV.h, h_done[ic - 1] - 1)
                ins = s.activation(c.SB[:, 0:2048], c.psA[:, 0:2048], AF.Silu,
                                   bias=0.0, scale=c.SCL[:, 62 + ic:63 + ic])
                c.sS.inc(ins)
                s.wait_ge(c.sT.h, gm_done[(ic, "u")])
                ins = s.activation(c.SB[:, 2048:4096], c.psB[:, 0:2048], AF.Identity,
                                   bias=0.0, scale=c.SCL[:, 22 + ic:23 + ic])
                c.sS.inc(ins)

        @blk.vector
        def _(v):
            v.wait_ge(c.sQ.h, c.spill_q)
            for ic in range(ICN):
                v.wait_ge(c.sS.h, ue_done[ic])
                ins = v.tensor_tensor(h_slot(ic), c.SB[:, 0:2048], c.SB[:, 2048:4096], OP.mult)
                c.sV.inc(ins)
                ins = v.reduce_max(c.SCL[:, 84 + ic:85 + ic],
                                   h_slot(ic), axis=mybir.AxisListType.X, apply_absolute_value=True)
                c.sV.inc(ins)

    if upto == "l0f_gu":
        return
    with nc.Block(no_gpsimd_drain=True) as blk:
        bV, bT, bQ, bG, bS = c.sV.n, c.sT.n, c.sQ.n, c.sG.n, c.sS.n

        @blk.tensor
        def _(t):
            t.wait_ge(c.sV.h, bV)
            ins = t.transpose(c.psA[0:22, 0:128], c.SCL[:, 84:106], c.IDT[:, :])
            c.sT.inc(ins)

        @blk.scalar
        def _(s):
            s.wait_ge(c.sT.h, c.sT.n)
            ins = s.activation(c.SBf[0:22, 0:128], c.psA[0:22, 0:128], AF.Identity, bias=0.0, scale=1.0)
            c.sS.inc(ins)

        @blk.vector
        def _(v):
            v.wait_ge(c.sS.h, c.sS.n)
            ins = v.reduce_max(c.SBf[0:22, 128:129], c.SBf[0:22, 0:128], axis=mybir.AxisListType.X)
            ins = v.memset(c.R32[0:1, 0:32], 0.0)
            c.sV.inc(ins)

        @blk.sync
        def _(sync):
            sync.wait_ge(c.sV.h, c.sV.n)
            ins = sync.dma_start(out=c.R32[0:1, 0:22], in_=c.SBf[0:22, 128:129])
            c.sQ.inc(ins, 16)

        @blk.gpsimd
        def _(g):
            g.wait_ge(c.sQ.h, c.sQ.n)
            ins = g.dma_start(out=c.d_cc2i.ap(), in_=c.R32[0:1, 0:32])
            c.sU.inc(ins, 16)
            g.wait_ge(c.sU.h, c.sU.n)
            ins = g.collective_compute("AllReduce", OP.max, replica_groups=[list(range(NCORES))],
                                       ins=[c.d_cc2i.ap()], outs=[c.d_cc2o.ap()])
            c.sC.inc(ins)
            g.wait_ge(c.sC.h, c.sC.n)
            ins = g.dma_start(out=c.R32[0:1, 0:32], in_=c.d_cc2o.ap())
            c.sU.inc(ins, 16)

        @blk.vector
        def _(v):
            v.wait_ge(c.sU.h, c.sU.n)
            ins = v.reduce_max(c.SBf[0:1, 129:130], c.R32[0:1, 0:32], axis=mybir.AxisListType.X)
            c.sV.inc(ins)

        @blk.scalar
        def _(s):
            s.wait_ge(c.sV.h, c.sV.n)
            ins = s.activation(c.SBf[0:1, 131:132], c.CN[0:1, 7:8], AF.Sigmoid, bias=0.0, scale=1.0)
            c.sS.inc(ins)

        @blk.vector
        def _(v):
            v.wait_ge(c.sS.h, c.sS.n)
            ins = v.tensor_tensor(c.SBf[0:1, 132:133], c.SBf[0:1, 129:130], c.SBf[0:1, 131:132], OP.mult)
            c.sV.inc(ins)

        @blk.scalar
        def _(s):
            s.wait_ge(c.sV.h, c.sV.n)
            ins = s.activation(c.SBf[0:1, 134:135], c.SBf[0:1, 132:133], AF.Identity,
                               bias=0.0, scale=c.CN[0:1, 4:5])
            c.sS.inc(ins)

        @blk.vector
        def _(v):
            v.wait_ge(c.sS.h, c.sS.n)
            ins = v.reciprocal(c.SBf[0:1, 133:134], c.SBf[0:1, 134:135])
            c.sV.inc(ins)

        @blk.tensor
        def _(t):
            t.wait_ge(c.sV.h, c.sV.n)
            ins = t.matmul(c.psB[:, 1024:1025], c.ONES[0:1, :], c.SBf[0:1, 134:135], start=True, stop=True)
            ins = t.matmul(c.psB[:, 1025:1026], c.ONES[0:1, :], c.SBf[0:1, 133:134], start=True, stop=True)
            c.sT.inc(ins)

        @blk.scalar
        def _(s):
            s.wait_ge(c.sT.h, c.sT.n)
            ins = s.activation(c.SCL[:, 106:108], c.psB[:, 1024:1026], AF.Identity, bias=0.0, scale=1.0)
            c.sS.inc(ins)

        @blk.vector
        def _(v):
            v.wait_ge(c.sS.h, c.sS.n)
            ins = v.tensor_scalar(c.SCL[:, 108:116], c.SCL[:, 44:52], c.SCL[:, 106:107], 1.0 / 127.0,
                                  OP.mult, OP.mult)
            c.sV.inc(ins)

    with nc.Block(no_gpsimd_drain=True) as blk:
        @blk.vector
        def _(v):
            for ic in range(ICN):
                ins = _quant4(c, v, h_slot(ic), h_slot(ic), 2048, c.SCL[:, 107:108], False)
                c.sV.inc(ins)

    if upto == "l0f_kh":
        return
    with nc.Block(no_gpsimd_drain=True) as blk:
        dl_done, dm_done, dx_done, dz_done, ds_done = {}, {}, {}, {}, {}
        nU, nV, nT, nQ = c.sU.n, c.sV.n, c.sT.n, c.sQ.n
        for ho in range(EC):
            nU += 16
            dl_done[ho] = nU
            nQ += 16
            dx_done[ho] = nQ
            nT += 1
            dm_done[ho] = nT
            nV += 1
            dz_done[ho] = nV
            nQ += 16
            ds_done[ho] = nQ

        SCb = c.SC[:, :].bitcast(BF16)

        @blk.gpsimd
        def _(g):
            for ho in range(EC):
                if ho >= 2:
                    g.wait_ge(c.sT.h, dm_done[ho - 2])
                src = bass.AP(c.d_kwd, ho * II * 128, [[128, 128], [128 * 128, ICN], [1, 128]])
                ins = g.dma_start(out=SCb[:, (ho % 2) * 2816:(ho % 2) * 2816 + ICN * 128], in_=src)
                c.sU.inc(ins, 16)

        @blk.sync
        def _(sync):
            for ho in range(EC):
                if ho >= 1:
                    sync.wait_ge(c.sQ.h, ds_done[ho - 1])
                ins = sync.dma_start(out=c.SBf[:, 0:2048], in_=c.d_xsp[ho * 128:(ho + 1) * 128, :])
                c.sQ.inc(ins, 16)
                sync.wait_ge(c.sV.h, dz_done[ho])
                ins = sync.dma_start(out=c.d_xsp[ho * 128:(ho + 1) * 128, :], in_=c.SBf[:, 0:2048])
                c.sQ.inc(ins, 16)

        @blk.tensor
        def _(t):
            for ho in range(EC):
                t.wait_ge(c.sU.h, dl_done[ho])
                ps = c.psA if ho % 2 == 0 else c.psB
                if ho >= 2:
                    t.wait_ge(c.sV.h, dz_done[ho - 2])
                for ic in range(ICN):
                    for nt in range(4):
                        ins = t.matmul(ps[:, nt * 512:(nt + 1) * 512],
                                       SCb[:, (ho % 2) * 2816 + ic * 128:(ho % 2) * 2816 + ic * 128 + 128],
                                       h_slot(ic)[:, nt * 512:nt * 512 + 512],
                                       start=(ic == 0), stop=(ic == ICN - 1))
                c.sT.inc(ins)

        @blk.vector
        def _(v):
            for ho in range(EC):
                v.wait_ge(c.sT.h, dm_done[ho])
                v.wait_ge(c.sQ.h, dx_done[ho])
                ps = c.psA if ho % 2 == 0 else c.psB
                ins = v.scalar_tensor_tensor(c.SBf[:, 0:2048], ps[:, 0:2048], c.SCL[:, 108 + ho:109 + ho],
                                             c.SBf[:, 0:2048], OP.mult, OP.add)
                c.sV.inc(ins)

    with nc.Block(no_gpsimd_drain=True) as blk:
        @blk.sync
        def _(sync):
            sync.wait_ge(c.sQ.h, c.sQ.n)
            for ch in range(EC):
                sync.dma_start(out=c.Xc(ch), in_=c.d_xsp[ch * 128:(ch + 1) * 128, :]).then_inc(c.sD.h, 16)
            c.sD.n += 8 * 16

        @blk.vector
        def _(v):
            v.wait_ge(c.sD.h, c.sD.n)
            ins = v.tensor_copy(c.SBf[0:1, 0:1], c.Xc(0)[0:1, 0:1])
            c.sV.inc(ins)


XN2 = 0
QKB = 16384
VT1 = 18432
W1Q = 26624
AOT = 32832
VBF = 16384


def p4_l1_attn(c, upto="full"):
    nc = c.nc
    B = c.BIGB

    with nc.Block(no_gpsimd_drain=True) as blk:
        @blk.gpsimd
        def _(g):
            for ch in range(EC):
                ins = g.dma_start(out=B[:, W1Q + ch * 1536:W1Q + (ch + 1) * 1536],
                                  in_=c.d_wqkv1[ch * 128:(ch + 1) * 128, :])
                c.sU.inc(ins, 16)
            for fc in range(4):
                ins = g.dma_start(out=c.MASK[:, fc * 1024:(fc + 1) * 1024],
                                  in_=c.d_wfc1[fc * 128:(fc + 1) * 128, :])
                c.sU.inc(ins, 16)

        @blk.sync
        def _(sync):
            sync.dma_start(out=c.BVR[:, :], in_=c.d_bq1r[:, :]).then_inc(c.sW.h, 16)
            sync.dma_start(out=c.CV1[:, 0:4], in_=c.d_bv1c[:, :]).then_inc(c.sW.h, 16)
            sync.dma_start(out=c.CV1[:, 4:12], in_=c.d_bfc1c[:, :]).then_inc(c.sW.h, 16)
            sync.dma_start(out=c.CV1[:, 12:20], in_=c.d_gam1[:, :]).then_inc(c.sW.h, 16)
            sync.dma_start(out=c.CV1[:, 20:28], in_=c.d_bet1[:, :]).then_inc(c.sW.h, 16)
            c.sW.n += 5 * 16

    emit_ln(c, 1e-5, sq_base=QKB)

    with nc.Block(no_gpsimd_drain=True) as blk:
        @blk.vector
        def _(v):
            emit_xn_cols(c, v, 0, 2048, XN2)

    with nc.Block(no_gpsimd_drain=True) as blk:
        qm_done, qe_done, sc_done, vm_done, ve_done = {}, {}, {}, {}, {}
        nV, nT, nS = c.sV.n, c.sT.n, c.sS.n
        for tc in range(16):
            nT += 1
            qm_done[tc] = nT
            nS += 1
            qe_done[tc] = nS
            nT += 1
            sc_done[tc] = nT
        for idx in range(16):
            nT += 1
            vm_done[idx] = nT
            nS += 1
            ve_done[idx] = nS

        @blk.tensor
        def _(t):
            t.wait_ge(c.sU.h, c.sU.n)
            t.wait_ge(c.sW.h, c.sW.n)
            for tc in range(16):
                if tc >= 2:
                    t.wait_ge(c.sS.h, qe_done[tc - 2])
                pslot = c.psA[:, (tc % 2) * 1024:(tc % 2) * 1024 + 1024]
                for nt in range(2):
                    for e in range(EC):
                        ins = t.matmul(pslot[:, nt * 512:(nt + 1) * 512],
                                       B[:, XN2 + e * 2048 + 128 * tc:XN2 + e * 2048 + 128 * tc + 128],
                                       B[:, W1Q + e * 1536 + nt * 512:W1Q + e * 1536 + nt * 512 + 512],
                                       start=(e == 0), stop=False, skip_group_check=True)
                    ins = t.matmul(pslot[:, nt * 512:(nt + 1) * 512], c.ONES[0:1, :],
                                   c.BVR[0:1, nt * 512:nt * 512 + 512],
                                   start=False, stop=True, skip_group_check=True)
                c.sT.inc(ins)
                assert c.sT.n == qm_done[tc]
                t.wait_ge(c.sS.h, qe_done[tc])
                qslot = B[:, QKB + (tc % 2) * 1024:QKB + (tc % 2) * 1024 + 1024]
                for h in range(8):
                    ins = t.matmul(c.psB[0:64, 1536 + 64 * h:1536 + 64 * h + 64],
                                   qslot[:, 64 * h:64 * h + 64], qslot[:, 512 + 64 * h:512 + 64 * h + 64],
                                   start=(tc == 0), stop=(tc == 15), skip_group_check=True)
                c.sT.inc(ins)
                assert c.sT.n == sc_done[tc]
            for idx in range(16):
                vc, tt = idx // 4, idx % 4
                if idx >= 2:
                    t.wait_ge(c.sS.h, ve_done[idx - 2])
                pslot = c.psB[:, (idx % 2) * 512:(idx % 2) * 512 + 512]
                for e in range(EC):
                    ins = t.matmul(pslot,
                                   B[:, W1Q + e * 1536 + 1024 + 128 * vc:W1Q + e * 1536 + 1024 + 128 * vc + 128],
                                   B[:, XN2 + e * 2048 + 512 * tt:XN2 + e * 2048 + 512 * tt + 512],
                                   start=(e == 0), stop=(e == EC - 1), skip_group_check=True)
                c.sT.inc(ins)
                assert c.sT.n == vm_done[idx]

        @blk.scalar
        def _(s):
            for tc in range(16):
                s.wait_ge(c.sT.h, qm_done[tc])
                if tc >= 2:
                    t_prev = sc_done[tc - 2]
                    s.wait_ge(c.sT.h, t_prev)
                pslot = c.psA[:, (tc % 2) * 1024:(tc % 2) * 1024 + 1024]
                ins = s.activation(B[:, QKB + (tc % 2) * 1024:QKB + (tc % 2) * 1024 + 1024],
                                   pslot, AF.Copy, bias=0.0, scale=1.0)
                c.sS.inc(ins)
                assert c.sS.n == qe_done[tc]
            for idx in range(16):
                vc, tt = idx // 4, idx % 4
                s.wait_ge(c.sT.h, vm_done[idx])
                pslot = c.psB[:, (idx % 2) * 512:(idx % 2) * 512 + 512]
                ins = s.activation(B[:, VT1 + vc * 2048 + 512 * tt:VT1 + vc * 2048 + 512 * tt + 512],
                                   pslot, AF.Identity, bias=c.CV1[:, vc:vc + 1], scale=1.0)
                c.sS.inc(ins)
                assert c.sS.n == ve_done[idx]

    with nc.Block(no_gpsimd_drain=True) as blk:
        bV0, bS0 = c.sV.n, c.sS.n

        @blk.vector
        def _(v):
            for h in range(8):
                ins = v.reduce_max(c.SBf[0:64, 1792 + h:1793 + h],
                                   c.psB[0:64, 1536 + 64 * h:1536 + 64 * h + 64],
                                   axis=mybir.AxisListType.X, negate=True)
                c.sV.inc(ins)

        @blk.scalar
        def _(s):
            for h in range(8):
                s.wait_ge(c.sV.h, bV0 + h + 1)
                ins = s.activation(c.SBf[0:64, 1800 + h:1801 + h], c.SBf[0:64, 1792 + h:1793 + h],
                                   AF.Identity, bias=0.0, scale=c.CN[0:1, 8:9] if False else 0.125)
                c.sS.inc(ins)
            for h in range(8):
                ins = s.activation(c.SB[0:64, 512 + 64 * h:512 + 64 * h + 64],
                                   c.psB[0:64, 1536 + 64 * h:1536 + 64 * h + 64],
                                   AF.Exp, bias=c.SBf[0:64, 1800 + h:1801 + h], scale=0.125,
                                   accum_out=c.SBf[0:64, 1816 + h:1817 + h])
                c.sS.inc(ins)

        @blk.vector
        def _(v):
            v.wait_ge(c.sS.h, c.sS.n)
            for h in range(8):
                ins = v.reciprocal(c.SBf[0:64, 1824 + h:1825 + h], c.SBf[0:64, 1816 + h:1817 + h])
                c.sV.inc(ins)

        @blk.scalar
        def _(s):
            s.wait_ge(c.sV.h, c.sV.n)
            for h in range(8):
                ins = s.activation(c.SB[0:64, 64 * h:64 * h + 64], c.SB[0:64, 512 + 64 * h:512 + 64 * h + 64],
                                   AF.Identity, bias=0.0, scale=c.SBf[0:64, 1824 + h:1825 + h])
                c.sS.inc(ins)

        @blk.sync
        def _(sync):
            sync.wait_ge(c.sS.h, c.sS.n)
            ins = sync.dma_start(out=c.SB[64:128, 0:512], in_=c.SB[0:64, 0:512])
            c.sQ.inc(ins, 16)

    with nc.Block(no_gpsimd_drain=True) as blk:
        ao_done, ae_done = {}, {}
        nT, nS = c.sT.n, c.sS.n
        for h in range(8):
            nT += 1
            ao_done[h] = nT
            nS += 1
            ae_done[h] = nS

        @blk.tensor
        def _(t):
            t.wait_ge(c.sQ.h, c.sQ.n)
            for h in range(8):
                off = 64 * (h % 2)
                ps = (c.psA if h % 2 == 0 else c.psB)
                if h >= 2:
                    t.wait_ge(c.sS.h, ae_done[h - 2])
                for nt in range(4):
                    ins = t.matmul(ps[off:off + 64, nt * 512:(nt + 1) * 512],
                                   c.SB[off:off + 64, 64 * h:64 * h + 64],
                                   bass.AP(B, off * BS + VT1 + (h // 2) * 2048 + nt * 512, [[BS, 64], [1, 512]]),
                                   start=True, stop=True, skip_group_check=True)
                c.sT.inc(ins)
                assert c.sT.n == ao_done[h]

        @blk.scalar
        def _(s):
            for h in range(8):
                s.wait_ge(c.sT.h, ao_done[h])
                off = 64 * (h % 2)
                ps = (c.psA if h % 2 == 0 else c.psB)
                ins = s.activation(bass.AP(B, off * BS + AOT + (h // 2) * 2048, [[BS, 64], [1, 2048]]),
                                   ps[off:off + 64, 0:2048], AF.Copy, bias=0.0, scale=1.0)
                c.sS.inc(ins)
                assert c.sS.n == ae_done[h]

    with nc.Block(no_gpsimd_drain=True) as blk:
        fm_done, fe_done, vb_done = {}, {}, {}
        nT, nS, nV = c.sT.n, c.sS.n, c.sV.n
        for oc in range(EC):
            nT += 1
            fm_done[oc] = nT
            nS += 1
            fe_done[oc] = nS
            nV += 1
            vb_done[oc] = nV

        @blk.tensor
        def _(t):
            t.wait_ge(c.sS.h, c.sS.n)
            for oc in range(EC):
                ps = (c.psA if oc % 2 == 0 else c.psB)
                if oc >= 2:
                    t.wait_ge(c.sS.h, fe_done[oc - 2])
                for nt in range(4):
                    for fc in range(4):
                        ins = t.matmul(ps[:, nt * 512:(nt + 1) * 512],
                                       c.MASK[:, fc * 1024 + 128 * oc:fc * 1024 + 128 * oc + 128],
                                       B[:, AOT + fc * 2048 + nt * 512:AOT + fc * 2048 + nt * 512 + 512],
                                       start=(fc == 0), stop=(fc == 3), skip_group_check=True)
                c.sT.inc(ins)
                assert c.sT.n == fm_done[oc]

        @blk.scalar
        def _(s):
            for oc in range(EC):
                s.wait_ge(c.sT.h, fm_done[oc])
                if oc >= 1:
                    s.wait_ge(c.sV.h, vb_done[oc - 1])
                ps = (c.psA if oc % 2 == 0 else c.psB)
                ins = s.activation(c.SB[:, 0:2048], ps[:, 0:2048], AF.Identity,
                                   bias=c.CV1[:, 4 + oc:5 + oc], scale=1.0)
                c.sS.inc(ins)
                assert c.sS.n == fe_done[oc]

        @blk.vector
        def _(v):
            for oc in range(EC):
                v.wait_ge(c.sS.h, fe_done[oc])
                ins = v.tensor_tensor(B[:, VBF + oc * 2048:VBF + (oc + 1) * 2048],
                                      c.SB[:, 0:2048], B[:, XN2 + oc * 2048:XN2 + (oc + 1) * 2048], OP.add)
                c.sV.inc(ins)
                assert c.sV.n == vb_done[oc]


    emit_ln(c, 1e-6, x_src=lambda ch: B[:, VBF + ch * 2048:VBF + (ch + 1) * 2048], sq_base=AOT)
    with nc.Block(no_gpsimd_drain=True) as blk:
        @blk.vector
        def _(v):
            for ch in range(EC):
                gcol = c.CV1[:, 12 + ch:13 + ch]
                bcol = c.CV1[:, 20 + ch:21 + ch]
                ins = v.tensor_tensor(c.SBf[:, 0:2048], B[:, VBF + ch * 2048:VBF + (ch + 1) * 2048],
                                      c.SC[:, 2048:4096], OP.mult)
                ins = v.tensor_tensor(c.SBf[:, 0:2048], c.SBf[:, 0:2048], c.SC[:, 0:2048], OP.subtract)
                ins = v.tensor_scalar(c.SB[:, 0:2048], c.SBf[:, 0:2048], gcol, bcol, OP.mult, OP.add)
                ins = v.tensor_tensor(c.Xc(ch), c.Xc(ch), c.SB[:, 0:2048], OP.add)
                c.sV.inc(ins)


HT1 = 0
YQ0 = 16384
WS0 = 32768


def p5_l1_ffn(c, upto="full"):
    nc = c.nc
    B = c.BIGB

    with nc.Block(no_gpsimd_drain=True) as blk:
        bU, bV, bT, bS, bQ = c.sU.n, c.sV.n, c.sT.n, c.sS.n, c.sQ.n

        @blk.sync
        def _(sync):
            sync.dma_start(out=c.CV1[:, 0:32], in_=c.d_b1c[:, :]).then_inc(c.sW.h, 16)
            sync.dma_start(out=c.BVR[:, :], in_=c.d_b2r[:, :]).then_inc(c.sW.h, 16)
            sync.dma_start(out=c.CV1[:, 32:40], in_=c.d_lngc[:, :]).then_inc(c.sW.h, 16)
            sync.dma_start(out=c.CV1[:, 40:48], in_=c.d_lnbc[:, :]).then_inc(c.sW.h, 16)
            c.sW.n += 4 * 16

        nT2, nV2, nS2, nQ2, nU2 = c.sT.n, c.sV.n, c.sS.n, c.sQ.n, c.sU.n
        plan = []
        for i in range(EC + FC):
            nQ2 += 16
            nU2 += 16
            nT2 += 1
            nS2 += 1
            nV2 += 2
            nQ2 += 16
            plan.append((nU2, nT2, nS2, nV2, nQ2, nQ2 - 16))

        @blk.gpsimd
        def _(g):
            g.wait_ge(c.sW.h, c.sW.n)
            for i, (hc) in enumerate(range(EC)):
                if i >= 2:
                    g.wait_ge(c.sQ.h, plan[i - 2][4])
                ins = g.dma_start(out=B[:, (i % 2) * 4096:(i % 2) * 4096 + 4096],
                                  in_=c.d_q1[hc * 128:(hc + 1) * 128, :])
                c.sU.inc(ins, 16)
            for i2, fc in enumerate(range(FC)):
                i = EC + i2
                if i >= 2:
                    g.wait_ge(c.sQ.h, plan[i - 2][4])
                ins = g.dma_start(out=B[:, (i % 2) * 4096:(i % 2) * 4096 + 1024],
                                  in_=c.d_q2[fc * 128:(fc + 1) * 128, :])
                c.sU.inc(ins, 16)

        @blk.sync
        def _(sync):
            for i in range(EC + FC):
                w = 4096 if i < EC else 1024
                if i >= 1:
                    sync.wait_ge(c.sT.h, plan[i - 1][1])
                if i >= 2:
                    sync.wait_ge(c.sV.h, plan[i - 2][3])
                src = c.d_s1[i:i + 1, :] if i < EC else c.d_s2[i - EC:i - EC + 1, :]
                ins = sync.dma_start(out=c.SC[0:1, 0:w], in_=src)
                c.sQ.inc(ins, 16)
                sync.wait_ge(c.sV.h, plan[i][3])
                slot = B[:, (i % 2) * 4096:(i % 2) * 4096 + w]
                if i < EC:
                    ins = sync.dma_start(out=c.d_w1d[i * 128:(i + 1) * 128, :], in_=slot)
                else:
                    ins = sync.dma_start(out=c.d_w2d[(i - EC) * 128:(i - EC + 1) * 128, :], in_=slot)
                c.sQ.inc(ins, 16)
                assert c.sQ.n == plan[i][4]

        @blk.tensor
        def _(t):
            for i in range(EC + FC):
                w = 4096 if i < EC else 1024
                t.wait_ge(c.sQ.h, plan[i][5])
                if i >= 2:
                    t.wait_ge(c.sS.h, plan[i - 2][2])
                off = 0
                while off < w:
                    wdt = min(512, w - off)
                    dstp = c.psA[:, off:off + wdt] if off < 2048 else c.psB[:, off - 2048:off - 2048 + wdt]
                    ins = t.matmul(dstp, c.ONES[0:1, :], c.SC[0:1, off:off + wdt],
                                   start=True, stop=True, skip_group_check=True)
                    off += wdt
                c.sT.inc(ins)
                assert c.sT.n == plan[i][1]

        @blk.scalar
        def _(s):
            for i in range(EC + FC):
                w = 4096 if i < EC else 1024
                s.wait_ge(c.sT.h, plan[i][1])
                ins = s.activation(c.SC[:, 0:min(w, 2048)], c.psA[:, 0:min(w, 2048)],
                                   AF.Identity, bias=0.0, scale=1.0)
                if w > 2048:
                    ins = s.activation(c.SC[:, 2048:w], c.psB[:, 0:w - 2048],
                                       AF.Identity, bias=0.0, scale=1.0)
                c.sS.inc(ins)
                assert c.sS.n == plan[i][2]

        @blk.vector
        def _(v):
            for i in range(EC + FC):
                w = 4096 if i < EC else 1024
                v.wait_ge(c.sU.h, plan[i][0])
                v.wait_ge(c.sS.h, plan[i][2])
                if i >= 2:
                    v.wait_ge(c.sQ.h, plan[i - 2][4])
                slot = B[:, (i % 2) * 4096:(i % 2) * 4096 + w]
                ins = v.tensor_scalar(slot, slot, -128.0, None, OP.add)
                ins = v.tensor_tensor(slot, slot, c.SC[:, 0:w], OP.mult)
                c.sV.inc(ins, 2)
                assert c.sV.n == plan[i][3]

    emit_ln(c, 1e-5, sq_base=YQ0)
    with nc.Block(no_gpsimd_drain=True) as blk:
        @blk.vector
        def _(v):
            emit_xn_cols(c, v, 0, 2048, HT1)
            for ch in range(EC):
                ins = v.tensor_tensor(c.Xc(ch), c.Xc(ch), B[:, HT1 + ch * 2048:HT1 + (ch + 1) * 2048], OP.add)
                c.sV.inc(ins)

    emit_ln(c, 1e-5, x_src=lambda ch: B[:, HT1 + ch * 2048:HT1 + (ch + 1) * 2048], sq_base=YQ0)
    with nc.Block(no_gpsimd_drain=True) as blk:
        @blk.vector
        def _(v):
            for ch in range(EC):
                gcol = c.CV1[:, 32 + ch:33 + ch]
                bcol = c.CV1[:, 40 + ch:41 + ch]
                ins = v.tensor_tensor(c.SBf[:, 0:2048], B[:, HT1 + ch * 2048:HT1 + (ch + 1) * 2048],
                                      c.SC[:, 2048:4096], OP.mult)
                ins = v.tensor_tensor(c.SBf[:, 0:2048], c.SBf[:, 0:2048], c.SC[:, 0:2048], OP.subtract)
                ins = v.tensor_scalar(B[:, HT1 + ch * 2048:HT1 + (ch + 1) * 2048],
                                      c.SBf[:, 0:2048], gcol, bcol, OP.mult, OP.add)
                c.sV.inc(ins)

    for q in range(4):
        with nc.Block(no_gpsimd_drain=True) as blk:
            w1l, f1m, f1e, w2l, f2m, f2e = {}, {}, {}, {}, {}, {}
            nU, nT, nS, nV = c.sU.n, c.sT.n, c.sS.n, c.sV.n
            for g in range(8):
                nU += 8 * 16
                w1l[g] = nU
            for fo in range(FC):
                nT += 1
                f1m[fo] = nT
                nS += 1
                f1e[fo] = nS
            for ho in range(EC):
                nU += 16
                w2l[ho] = nU
                nT += 1
                f2m[ho] = nT
                nV += 1
                f2e[ho] = nV

            @blk.gpsimd
            def _(g_):
                for g in range(8):
                    if g >= 2:
                        g_.wait_ge(c.sT.h, f1m[4 * g - 5])
                    for cc in range(EC):
                        ins = g_.dma_start(
                            out=B[:, WS0 + (g % 2) * 4096 + cc * 512:WS0 + (g % 2) * 4096 + cc * 512 + 512],
                            in_=c.d_w1d[cc * 128:(cc + 1) * 128, g * 512:(g + 1) * 512])
                        c.sU.inc(ins, 16)
                    assert c.sU.n == w1l[g]
                for ho in range(EC):
                    if ho == 0:
                        g_.wait_ge(c.sT.h, f1m[FC - 1])
                    if ho >= 2:
                        g_.wait_ge(c.sT.h, f2m[ho - 2])
                    src = bass.AP(c.d_w2d, 128 * ho, [[1024, 128], [128 * 1024, FC], [1, 128]])
                    ins = g_.dma_start(out=B[:, WS0 + (ho % 2) * 4096:WS0 + (ho % 2) * 4096 + FC * 128],
                                       in_=src)
                    c.sU.inc(ins, 16)
                    assert c.sU.n == w2l[ho]

            @blk.tensor
            def _(t):
                for fo in range(FC):
                    g = fo // 4
                    t.wait_ge(c.sU.h, w1l[g])
                    if fo >= 4:
                        t.wait_ge(c.sS.h, f1e[fo - 4])
                    ps = c.psA[:, (fo % 4) * 512:(fo % 4) * 512 + 512]
                    for cc in range(EC):
                        lhs = B[:, WS0 + (g % 2) * 4096 + cc * 512 + (fo % 4) * 128:
                                WS0 + (g % 2) * 4096 + cc * 512 + (fo % 4) * 128 + 128]
                        ins = t.matmul(ps, lhs,
                                       B[:, HT1 + cc * 2048 + q * 512:HT1 + cc * 2048 + q * 512 + 512],
                                       start=(cc == 0), stop=(cc == EC - 1), skip_group_check=True)
                    c.sT.inc(ins)
                    assert c.sT.n == f1m[fo]
                for ho in range(EC):
                    t.wait_ge(c.sU.h, w2l[ho])
                    if ho == 0:
                        t.wait_ge(c.sS.h, f1e[FC - 1])
                    if ho >= 4:
                        t.wait_ge(c.sV.h, f2e[ho - 4])
                    ps = c.psB[:, (ho % 4) * 512:(ho % 4) * 512 + 512]
                    for cc in range(FC):
                        lhs = B[:, WS0 + (ho % 2) * 4096 + cc * 128:WS0 + (ho % 2) * 4096 + cc * 128 + 128]
                        ins = t.matmul(ps, lhs,
                                       B[:, YQ0 + cc * 512:YQ0 + (cc + 1) * 512],
                                       start=(cc == 0), stop=False, skip_group_check=True)
                    ins = t.matmul(ps, c.BVR[0:1, 128 * ho:128 * ho + 128], c.ONER[0:1, :],
                                   start=False, stop=True, skip_group_check=True)
                    c.sT.inc(ins)
                    assert c.sT.n == f2m[ho]

            @blk.scalar
            def _(s):
                for fo in range(FC):
                    s.wait_ge(c.sT.h, f1m[fo])
                    ps = c.psA[:, (fo % 4) * 512:(fo % 4) * 512 + 512]
                    ins = s.activation(B[:, YQ0 + fo * 512:YQ0 + (fo + 1) * 512], ps, AF.Silu,
                                       bias=c.CV1[:, fo:fo + 1], scale=1.0)
                    c.sS.inc(ins)
                    assert c.sS.n == f1e[fo]

            @blk.vector
            def _(v):
                for ho in range(EC):
                    v.wait_ge(c.sT.h, f2m[ho])
                    ps = c.psB[:, (ho % 4) * 512:(ho % 4) * 512 + 512]
                    xsl = c.Xc(ho)[:, q * 512:q * 512 + 512]
                    ins = v.scalar_tensor_tensor(xsl, ps, 0.5, xsl, OP.mult, OP.add)
                    c.sV.inc(ins)
                    assert c.sV.n == f2e[ho]
